# version 21 (final)
# baseline (speedup 1.0000x reference)
import numpy as np
import ml_dtypes
from contextlib import ExitStack

import concourse.bass as bass
import concourse.mybir as mybir
from concourse.bass_utils import run_bass_kernel_spmd

F32 = mybir.dt.float32
BF16 = mybir.dt.bfloat16
AF = mybir.ActivationFunctionType
ALU = mybir.AluOpType
AX = mybir.AxisListType

D = 1024
NCORES = 8
EPS = 1e-5
SCALE = 1.0 / 32.0
NEG_BIG = -1.0e30

ENGINES = ("sync", "tensor", "vector", "scalar", "gpsimd")


class Plan:

    def __init__(self):
        self.ops = {e: [] for e in ENGINES}
        self.cnt = {}
        self.wm = {e: {} for e in ENGINES}
        self.wr = {}
        self.rd = {}

    @staticmethod
    def _expand(key, lo, hi):
        if key.startswith("P"):
            return (lo // 512) * 512, ((hi + 511) // 512) * 512
        return lo, hi

    def emit(self, eng, fn, reads=(), writes=(), sem=None, inc=0, waits=(),
             marker_override=None):
        need = {}
        for (k, lo, hi) in reads:
            lo, hi = self._expand(k, lo, hi)
            for (l2, h2, s2, v2) in self.wr.get(k, ()):
                if l2 < hi and lo < h2:
                    need[s2] = max(need.get(s2, 0), v2)
        for (k, lo, hi) in writes:
            lo, hi = self._expand(k, lo, hi)
            for src in (self.wr.get(k, ()), self.rd.get(k, ())):
                for (l2, h2, s2, v2) in src:
                    if l2 < hi and lo < h2:
                        need[s2] = max(need.get(s2, 0), v2)
        for (s2, v2) in waits:
            need[s2] = max(need.get(s2, 0), v2)

        wmap = self.wm[eng]
        wl = []
        for s2, v2 in sorted(need.items()):
            if wmap.get(s2, 0) < v2:
                wl.append((s2, v2))
                wmap[s2] = v2
        assert inc, "every op must carry an inc"
        self.cnt[sem] = self.cnt.get(sem, 0) + inc
        mval = self.cnt[sem] if marker_override is None else marker_override
        marker = (sem, mval)

        self.ops[eng].append((wl, fn, sem, inc))

        for (k, lo, hi) in writes:
            lo, hi = self._expand(k, lo, hi)
            lst = [e for e in self.wr.get(k, ()) if not (e[0] >= lo and e[1] <= hi)]
            lst.append((lo, hi, marker[0], marker[1]))
            self.wr[k] = lst
            self.rd[k] = [e for e in self.rd.get(k, ()) if not (e[0] >= lo and e[1] <= hi)]
        for (k, lo, hi) in reads:
            lo, hi = self._expand(k, lo, hi)
            self.rd.setdefault(k, []).append((lo, hi, marker[0], marker[1]))


def build_nc(S=4096):
    NT = S // 128
    NST = S // 512
    PB = max(S, 4096)
    nc = bass.Bass("TRN2", target_bir_lowering=False)

    x_d = nc.dram_tensor("x", [S, D], F32, kind="ExternalInput").ap()
    wqT_d = nc.dram_tensor("wqT", [D, D], F32, kind="ExternalInput").ap()
    wkT_d = nc.dram_tensor("wkT", [D, D], F32, kind="ExternalInput").ap()
    wvT_d = nc.dram_tensor("wvT", [D, D], F32, kind="ExternalInput").ap()
    wfT_d = nc.dram_tensor("wfT", [D, D], F32, kind="ExternalInput").ap()
    bias_d = nc.dram_tensor("fbias", [1, D], F32, kind="ExternalInput").ap()
    alpha_d = nc.dram_tensor("falpha", [1, 1], F32, kind="ExternalInput").ap()
    out_d = nc.dram_tensor("out", [S, D], F32, kind="ExternalOutput").ap()

    xnspill_d = nc.dram_tensor("xnspill", [NST, 128, 8 * 512], BF16, kind="Internal").ap()
    wftspill_d = nc.dram_tensor("wftspill", [128, 8 * D], BF16, kind="Internal").ap()

    ident_h = nc.inline_tensor(np.eye(128, dtype=ml_dtypes.bfloat16), "identc").ap()
    mbv = np.where(np.arange(128)[None, :] <= np.arange(128)[:, None], 0.0, NEG_BIG)
    mask_h = nc.inline_tensor(mbv.astype(ml_dtypes.bfloat16), "maskc").ap()
    onesc_h = nc.inline_tensor(np.ones((128, 1), dtype=np.float32), "onescc").ap()
    onesr_h = nc.inline_tensor(np.ones((1, 128), dtype=np.float32), "onesrc").ap()
    onesrb_h = nc.inline_tensor(np.ones((1, 128), dtype=ml_dtypes.bfloat16), "onesrbc").ap()

    p = Plan()

    with ExitStack() as es:
        def sb(name, shape, dtype):
            return es.enter_context(nc.sbuf_tensor(name, shape, dtype))

        Kb = sb("Kb", [128, 8 * S], BF16)
        Vb = sb("Vb", [128, NT * D], BF16)
        Wl = sb("Wl", [128, 2 * 8 * D], BF16)
        xnst = sb("xnst", [128, 8 * 512], BF16)
        qst = sb("qst", [128, 8 * 512], BF16)
        pbuf = sb("pbuf", [128, PB], BF16)
        pT = sb("pT", [128, 1536], BF16)
        xin = [sb("xin0", [128, D], F32), sb("xin1", [128, D], F32)]
        outb = [sb("outb0", [128, D], F32), sb("outb1", [128, D], F32)]
        biasbc = sb("biasbc", [128, D], BF16)
        maskb = sb("maskb", [128, 128], BF16)
        identb = sb("identb", [128, 128], BF16)
        onesc = sb("onesc", [128, 1], F32)
        onesr = sb("onesr", [1, 128], F32)
        onesrb = sb("onesrb", [1, 128], BF16)
        stats = sb("stats", [128, 32], F32)
        dsc = sb("dsc", [1, 1], F32)
        alsc = sb("alsc", [1, 1], F32)

        P = [es.enter_context(nc.psum_tensor(f"P{i}", [128, 1024], F32)) for i in range(4)]

        semnames = ("qc", "qbc", "qw0", "qw1", "qwv", "qwv2", "qf0", "qf1", "qb1", "qb2", "qx0", "qx1", "qxn",
                    "qsp", "qo0", "qo1", "st", "sv", "ss", "sg")
        sems = {n: es.enter_context(nc.semaphore(n)) for n in semnames}
        block = es.enter_context(nc.Block())

        C_ABS, C_ABS2 = 0, 1
        C_DELTA, C_NDELTA, C_ALPHA = 2, 3, 4
        LN_M = (5, 10)
        LN_SQ = (6, 11)
        LN_V = (7, 12)
        LN_R = (8, 13)
        C_M = 9
        C_MB0 = 14
        C_LB0 = 20
        C_RL = 28
        C_EPS = 29
        C_MX0 = 14

        def st_col(c):
            return stats[:, c:c + 1]

        def st_r(c):
            return ("stats", c, c + 1)

        p.emit("vector", lambda e: e.memset(st_col(C_EPS), EPS),
               writes=(st_r(C_EPS),), sem="sv", inc=1)
        for t0_ in (0, 1):
            def ldx0(e, t0_=t0_):
                return e.dma_start(xin[t0_][:, :], x_d[128 * t0_:128 * (t0_ + 1), :])
            p.emit("sync", ldx0, writes=((f"xin{t0_}", 0, D),), sem=f"qx{t0_}", inc=16)
        for c in range(8):
            def ldv(e, c=c):
                return e.dma_start(Wl[:, 8192 + 1024 * c:8192 + 1024 * (c + 1)], wvT_d[128 * c:128 * (c + 1), :])
            p.emit("gpsimd", ldv, writes=(("Wl", 8192 + 1024 * c, 8192 + 1024 * (c + 1)),),
                   sem="qwv2", inc=16, marker_override=128)
        for c in range(8):
            def ldk(e, c=c):
                return e.dma_start(Wl[:, 1024 * c:1024 * (c + 1)], wkT_d[128 * c:128 * (c + 1), :])
            p.emit("gpsimd", ldk, writes=(("Wl", 1024 * c, 1024 * (c + 1)),),
                   sem="qwv", inc=16, marker_override=128)
        assert p.cnt["qwv"] == 128 and p.cnt["qwv2"] == 128

        NCONST = 6
        p.emit("sync", lambda e: e.dma_start(identb[:, :], ident_h[:, :]),
               writes=(("identb", 0, 128),), sem="qc", inc=16, marker_override=16 * NCONST)
        p.emit("sync", lambda e: e.dma_start(maskb[:, :], mask_h[:, :]),
               writes=(("maskb", 0, 128),), sem="qc", inc=16, marker_override=16 * NCONST)
        p.emit("sync", lambda e: e.dma_start(onesc[:, :], onesc_h[:, :]),
               writes=(("onesc", 0, 1),), sem="qc", inc=16, marker_override=16 * NCONST)
        p.emit("sync", lambda e: e.dma_start(onesr[:, :], onesr_h[:, :]),
               writes=(("onesr", 0, 128),), sem="qc", inc=16, marker_override=16 * NCONST)
        p.emit("sync", lambda e: e.dma_start(onesrb[:, :], onesrb_h[:, :]),
               writes=(("onesrb", 0, 128),), sem="qc", inc=16, marker_override=16 * NCONST)
        p.emit("sync", lambda e: e.dma_start(alsc[:, :], alpha_d[:, :]),
               writes=(("alsc", 0, 1),), sem="qc", inc=16, marker_override=16 * NCONST)
        p.emit("gpsimd", lambda e: e.dma_start(biasbc[:1, :], bias_d[:, :]),
               writes=(("biasbc", 0, D),), sem="qbc", inc=16)
        assert p.cnt["qc"] == 16 * NCONST

        def emit_tern_p1(cs):
            for c in cs:
                ob = outb[c % 2]
                okey = f"outb{c % 2}"

                def ld(e, c=c, ob=ob):
                    return e.dma_start(ob[:, :], wfT_d[128 * c:128 * (c + 1), :])
                p.emit("sync", ld, writes=((okey, 0, D),), sem=f"qw{c % 2}", inc=16)
                dst = C_ABS if c == 0 else C_ABS2

                def red(e, dst=dst, ob=ob):
                    return e.reduce_sum(st_col(dst), ob[:, :], AX.X, apply_absolute_value=True)
                p.emit("vector", red, reads=((okey, 0, D),), writes=(st_r(dst),), sem="sv", inc=1)
                if c > 0:
                    p.emit("vector", lambda e: e.tensor_add(st_col(C_ABS), st_col(C_ABS), st_col(C_ABS2)),
                           reads=(st_r(C_ABS), st_r(C_ABS2)), writes=(st_r(C_ABS),), sem="sv", inc=1)

        def emit_tern_mid():
            p.emit("tensor", lambda e: e.matmul(P[0][:1, 0:1], onesc[:, :], st_col(C_ABS), start=True, stop=True),
                   reads=(("onesc", 0, 1), st_r(C_ABS)), writes=(("P0", 0, 1),), sem="st", inc=1)
            p.emit("vector", lambda e: e.tensor_copy(dsc[:, :], P[0][:1, 0:1]),
                   reads=(("P0", 0, 1),), writes=(("dsc", 0, 1),), sem="sv", inc=1)
            p.emit("tensor", lambda e: e.matmul(P[0][:, 0:1], onesr[:, :], dsc[:, :], start=True, stop=True),
                   reads=(("onesr", 0, 128), ("dsc", 0, 1)), writes=(("P0", 0, 1),), sem="st", inc=1)
            p.emit("vector", lambda e: e.tensor_scalar(st_col(C_DELTA), P[0][:, 0:1], 0.7 / (1024.0 * 1024.0), None, ALU.mult),
                   reads=(("P0", 0, 1),), writes=(st_r(C_DELTA),), sem="sv", inc=1)
            p.emit("vector", lambda e: e.tensor_scalar(st_col(C_NDELTA), st_col(C_DELTA), -1.0, None, ALU.mult),
                   reads=(st_r(C_DELTA),), writes=(st_r(C_NDELTA),), sem="sv", inc=1)
            p.emit("tensor", lambda e: e.matmul(P[0][:, 0:1], onesr[:, :], alsc[:, :], start=True, stop=True),
                   reads=(("onesr", 0, 128), ("alsc", 0, 1)), writes=(("P0", 0, 1),), sem="st", inc=1)
            p.emit("vector", lambda e: e.tensor_copy(st_col(C_ALPHA), P[0][:, 0:1]),
                   reads=(("P0", 0, 1),), writes=(st_r(C_ALPHA),), sem="sv", inc=1)
            for h in range(2):
                def bmm(e, h=h):
                    return e.matmul(P[1][:, 512 * h:512 * (h + 1)], onesrb[:, :],
                                    biasbc[:1, 512 * h:512 * (h + 1)], start=True, stop=True)
                p.emit("tensor", bmm, reads=(("onesrb", 0, 128), ("biasbc", 512 * h, 512 * h + 512)),
                       writes=(("P1", 512 * h, 512 * h + 512),), sem="st", inc=1)
            p.emit("vector", lambda e: e.tensor_copy(biasbc[:, :], P[1][:, :]),
                   reads=(("P1", 0, 1024),), writes=(("biasbc", 0, D),), sem="sv", inc=1)

        def emit_tern_p2(cs):
            for c in cs:
                ob = outb[c % 2]
                okey = f"outb{c % 2}"
                so_lo = 1024 + 1024 * (c % 2)

                def ld2(e, c=c, ob=ob):
                    return e.dma_start(ob[:, :], wfT_d[128 * c:128 * (c + 1), :])
                p.emit("sync", ld2, writes=((okey, 0, D),), sem=f"qw{c % 2}", inc=16)
                p.emit("vector", lambda e, ob=ob: e.tensor_scalar(pbuf[:, 2048:3072], ob[:, :], st_col(C_DELTA), None, ALU.is_gt),
                       reads=((okey, 0, D), st_r(C_DELTA)), writes=(("pbuf", 2048, 3072),), sem="sv", inc=1)
                p.emit("vector", lambda e, ob=ob: e.tensor_scalar(pbuf[:, 3072:4096], ob[:, :], st_col(C_NDELTA), None, ALU.is_lt),
                       reads=((okey, 0, D), st_r(C_NDELTA)), writes=(("pbuf", 3072, 4096),), sem="sv", inc=1)

                def sub(e, so_lo=so_lo):
                    return e.tensor_sub(qst[:, so_lo:so_lo + 1024], pbuf[:, 2048:3072], pbuf[:, 3072:4096])
                p.emit("vector", sub, reads=(("pbuf", 2048, 4096),), writes=(("qst", so_lo, so_lo + 1024),),
                       sem="sv", inc=1)

                def spw(e, c=c, so_lo=so_lo):
                    return e.dma_start(wftspill_d[:, 1024 * c:1024 * (c + 1)], qst[:, so_lo:so_lo + 1024])
                p.emit("gpsimd", spw, reads=(("qst", so_lo, so_lo + 1024),),
                       writes=(("wftspill", 1024 * c, 1024 * (c + 1)),), sem=f"qf{c % 2}", inc=16)

        xnst_v = xnst[:].rearrange("p (c t) -> p c t", c=8)

        def emit_lnA(t):
            par = t % 2
            xb = xin[par]
            xkey = f"xin{par}"
            xq = f"qx{par}"
            cm, cs, cv, cr = LN_M[par], LN_SQ[par], LN_V[par], LN_R[par]

            if t >= 2:
                def ldx(e, t=t, xb=xb):
                    return e.dma_start(xb[:, :], x_d[128 * t:128 * (t + 1), :])
                p.emit("sync", ldx, writes=((xkey, 0, D),), sem=xq, inc=16)

            def r1(e, xb=xb, cm=cm):
                return e.activation(qst[:, 0:D], xb[:, :], AF.Copy, accum_out=st_col(cm))
            p.emit("scalar", r1, reads=((xkey, 0, D),), writes=(("qst", 0, D), st_r(cm)), sem="ss", inc=1)
            p.emit("scalar", lambda e, cm=cm: e.mul(st_col(cm), st_col(cm), -1.0 / D),
                   reads=(st_r(cm),), writes=(st_r(cm),), sem="ss", inc=1)

            def sq(e, xb=xb, cm=cm, cs=cs):
                return e.activation(qst[:, 0:D], xb[:, :], AF.Square, bias=st_col(cm),
                                    scale=1.0, accum_out=st_col(cs))
            p.emit("scalar", sq, reads=((xkey, 0, D), st_r(cm)),
                   writes=(("qst", 0, D), st_r(cs)), sem="ss", inc=1)
            p.emit("scalar", lambda e, cs=cs, cv=cv: e.activation(st_col(cv), st_col(cs), AF.Ln,
                                                                  bias=st_col(C_EPS), scale=1.0 / D),
                   reads=(st_r(cs), st_r(C_EPS)), writes=(st_r(cv),), sem="ss", inc=1)
            p.emit("scalar", lambda e, cv=cv, cr=cr: e.activation(st_col(cr), st_col(cv), AF.Exp, bias=0.0, scale=-0.5),
                   reads=(st_r(cv),), writes=(st_r(cr),), sem="ss", inc=1)

        def emit_lnB(t):
            par = t % 2
            xb = xin[par]
            xkey = f"xin{par}"
            cm, cr = LN_M[par], LN_R[par]
            xtm_lo = 1024 * par

            def norm(e, xb=xb, cm=cm, cr=cr, xtm_lo=xtm_lo):
                return e.tensor_scalar(pbuf[:, xtm_lo:xtm_lo + D], xb[:, :], st_col(cm), st_col(cr), ALU.add, ALU.mult)
            p.emit("vector", norm, reads=((xkey, 0, D), st_r(cm), st_r(cr)),
                   writes=(("pbuf", xtm_lo, xtm_lo + D),), sem="sv", inc=1)
            return xtm_lo

        def emit_trs(t, tt, xtm_lo):
            par = t % 2
            pd = P[par]
            pdkey = f"P{par}"

            def trs(e, xtm_lo=xtm_lo, pd=pd):
                for c in range(8):
                    mm = e.matmul(pd[:, 128 * c:128 * (c + 1)], pbuf[:, xtm_lo + 128 * c:xtm_lo + 128 * (c + 1)],
                                  identb[:, :], start=True, stop=True)
                return mm
            p.emit("tensor", trs, reads=(("pbuf", xtm_lo, xtm_lo + D), ("identb", 0, 128)),
                   writes=((pdkey, 0, 1024),), sem="st", inc=1)

            xn_wr = tuple(("xnst", 512 * c + 128 * tt, 512 * c + 128 * (tt + 1)) for c in range(8))

            def evx(e, tt=tt, pd=pd):
                return e.tensor_copy(xnst_v[:, :, 128 * tt:128 * (tt + 1)],
                                     pd[:].rearrange("p (c t) -> p c t", c=8)[:, :, :])
            p.emit("vector", evx, reads=((pdkey, 0, 1024),), writes=xn_wr, sem="sv", inc=1)
            return xn_wr

        def emit_v(t, tt, xn_wr):
            par = t % 2
            pv = P[2 + par]
            pvkey = f"P{2 + par}"

            def vmm(e, tt=tt, pv=pv):
                for c in range(8):
                    for h in range(2):
                        mm = e.matmul(pv[:, 512 * h:512 * (h + 1)],
                                      xnst[:, 512 * c + 128 * tt:512 * c + 128 * (tt + 1)],
                                      Wl[:, 8192 + 1024 * c + 512 * h:8192 + 1024 * c + 512 * (h + 1)],
                                      start=(c == 0), stop=(c == 7))
                return mm
            p.emit("tensor", vmm, reads=xn_wr + (("Wl", 8192, 16384),),
                   writes=((pvkey, 0, 1024),), sem="st", inc=1)

            def evv(e, t=t, pv=pv):
                return e.copy(Vb[:, D * t:D * (t + 1)], pv[:, :])
            p.emit("scalar", evv, reads=((pvkey, 0, 1024),), writes=(("Vb", D * t, D * (t + 1)),),
                   sem="ss", inc=1)

        lnA_done = set()
        lnB_done = set()
        carryV = []
        for sti in range(NST):
            tiles = [4 * sti + k for k in range(4)]
            Bq = [t for t in tiles if t not in lnB_done]
            Vq = list(carryV)
            carryV = []
            for tt in range(4):
                t = tiles[tt]
                if t not in lnA_done:
                    emit_lnA(t)
                    lnA_done.add(t)
                if tt >= 1 and (len(Vq) >= 2 or (tt >= 2 and Vq)):
                    emit_v(*Vq.pop(0))
                if tt >= 1 and Bq:
                    tb = Bq.pop(0)
                    xtm_lo = emit_lnB(tb)
                    xn_wr = emit_trs(tb, tb % 4, xtm_lo)
                    Vq.append((tb, tb % 4, xn_wr))
                    lnB_done.add(tb)
                    if tb + 2 < 4 * NST and tb + 2 not in lnA_done:
                        emit_lnA(tb + 2)
                        lnA_done.add(tb + 2)
            while Bq:
                tb = Bq.pop(0)
                xtm_lo = emit_lnB(tb)
                xn_wr = emit_trs(tb, tb % 4, xtm_lo)
                Vq.append((tb, tb % 4, xn_wr))
                lnB_done.add(tb)
                if tb + 2 < 4 * NST and tb + 2 not in lnA_done:
                    emit_lnA(tb + 2)
                    lnA_done.add(tb + 2)
            while Vq:
                emit_v(*Vq.pop(0))

            def spl(e, sti=sti):
                return e.dma_start(xnspill_d[sti, :, :], xnst[:, :])
            p.emit("gpsimd", spl, reads=(("xnst", 0, 4096),),
                   writes=(("xnspill", 4096 * sti, 4096 * (sti + 1)),), sem="qsp", inc=16)

            if NST >= 8:
                if sti == 0:
                    emit_tern_p1(range(0, 4))
                elif sti == 1:
                    emit_tern_p1(range(4, 8))
                elif sti == 2:
                    emit_tern_mid()
                elif sti in (3, 4, 5, 6):
                    emit_tern_p2(range(2 * (sti - 3), 2 * (sti - 2)))
            else:
                if sti == NST - 1:
                    emit_tern_p1(range(0, 8))
                    emit_tern_mid()
                    emit_tern_p2(range(0, 8))

            for kd in range(8):
                bank = kd % 4
                pt_ = P[bank // 2]
                pkey = f"P{bank // 2}"
                off = 512 * (bank % 2)

                def kmm(e, kd=kd, pt_=pt_, off=off):
                    for c in range(8):
                        mm = e.matmul(pt_[:, off:off + 512],
                                      Wl[:, 1024 * c + 128 * kd:1024 * c + 128 * (kd + 1)],
                                      xnst[:, 512 * c:512 * (c + 1)],
                                      start=(c == 0), stop=(c == 7))
                    return mm
                p.emit("tensor", kmm, reads=(("xnst", 0, 4096), ("Wl", 0, 8192)),
                       writes=((pkey, off, off + 512),), sem="st", inc=1)

                def evk(e, kd=kd, sti=sti, pt_=pt_, off=off):
                    return e.copy(Kb[:, S * kd + 512 * sti:S * kd + 512 * (sti + 1)], pt_[:, off:off + 512])
                p.emit("scalar", evk, reads=((pkey, off, off + 512),),
                       writes=(("Kb", S * kd + 512 * sti, S * kd + 512 * (sti + 1)),), sem="ss", inc=1)

            if sti + 1 < NST:
                tn = 4 * sti + 4
                xtm_lo = emit_lnB(tn)
                xn_wr = emit_trs(tn, 0, xtm_lo)
                carryV.append((tn, 0, xn_wr))
                lnB_done.add(tn)
                if tn + 2 < 4 * NST and tn + 2 not in lnA_done:
                    emit_lnA(tn + 2)
                    lnA_done.add(tn + 2)


        for c in range(8):
            def ldq(e, c=c):
                return e.dma_start(Wl[:, 8192 + 1024 * c:8192 + 1024 * (c + 1)], wqT_d[128 * c:128 * (c + 1), :])
            p.emit("gpsimd", ldq, writes=(("Wl", 8192 + 1024 * c, 8192 + 1024 * (c + 1)),),
                   sem="qb2", inc=16, marker_override=128)
        assert p.cnt["qb2"] == 128
        p.emit("gpsimd", lambda e: e.dma_start(Wl[:, 0:8192], wftspill_d[:, :]),
               reads=(("wftspill", 0, 8192),), writes=(("Wl", 0, 8192),), sem="qb1", inc=16)

        for i in range(NT):
            sti, o = i // 4, i % 4
            L = 128 * (i + 1)
            nb = (L + 511) // 512

            if o == 0:
                if i == 0:
                    def ldxn(e, sti=sti):
                        return e.dma_start(xnst[:, :], xnspill_d[sti, :, :])
                    p.emit("sync", ldxn, reads=(("xnspill", 4096 * sti, 4096 * (sti + 1)),),
                           writes=(("xnst", 0, 4096),), sem="qxn", inc=16)
                for kd in range(8):
                    off = 512 * (kd % 2)

                    def qmm(e, kd=kd, off=off):
                        for c in range(8):
                            mm = e.matmul(P[3][:, off:off + 512],
                                          Wl[:, 8192 + 1024 * c + 128 * kd:8192 + 1024 * c + 128 * (kd + 1)],
                                          xnst[:, 512 * c:512 * (c + 1)],
                                          start=(c == 0), stop=(c == 7))
                        return mm
                    p.emit("tensor", qmm, reads=(("xnst", 0, 4096), ("Wl", 8192, 16384)),
                           writes=(("P3", off, off + 512),), sem="st", inc=1)

                    def evq(e, kd=kd, off=off):
                        return e.tensor_copy(qst[:, 512 * kd:512 * (kd + 1)], P[3][:, off:off + 512])
                    p.emit("vector", evq, reads=(("P3", off, off + 512),),
                           writes=(("qst", 512 * kd, 512 * (kd + 1)),), sem="sv", inc=1)

            xb = xin[i % 2]
            xkey = f"xin{i % 2}"
            xq = f"qx{i % 2}"
            ob = outb[i % 2]
            okey = f"outb{i % 2}"
            oq = f"qo{i % 2}"

            def ldx2(e, i=i, xb=xb):
                return e.dma_start(xb[:, :], x_d[128 * i:128 * (i + 1), :])
            p.emit("sync", ldx2, writes=((xkey, 0, D),), sem=xq, inc=16)

            q_rd = tuple(("qst", 512 * kd + 128 * o, 512 * kd + 128 * (o + 1)) for kd in range(8))
            bdiag = (L - 128) // 512
            njt = i + 1
            ng = (njt + 3) // 4
            assert ng == nb

            def blkpos(b):
                if b >= 4:
                    return 3, 512 * ((b - 4) % 2)
                if b == bdiag:
                    return 3, 512 * (b % 2)
                return b // 2, 512 * (b % 2)

            def emit_smm(b):
                w = min(512, L - 512 * b)
                ti, off = blkpos(b)
                pt_ = P[ti]
                pkey = f"P{ti}"
                dg = (b == bdiag)
                dof = off + (L - 128 - 512 * b) if dg else 0

                def smm(e, b=b, w=w, pt_=pt_, off=off, o=o, dg=dg, dof=dof):
                    for kd in range(8):
                        mm = e.matmul(pt_[:, off:off + w],
                                      qst[:, 512 * kd + 128 * o:512 * kd + 128 * (o + 1)],
                                      Kb[:, S * kd + 512 * b:S * kd + 512 * b + w],
                                      start=(kd == 0), stop=(kd == 7 and not dg))
                    if dg:
                        mm = e.matmul(pt_[:, dof:dof + 128], identb[:, :], maskb[:, :],
                                      start=False, stop=True)
                    return mm
                p.emit("tensor", smm,
                       reads=q_rd + tuple(("Kb", S * kd + 512 * b, S * kd + 512 * b + w) for kd in range(8))
                             + ((("identb", 0, 128), ("maskb", 0, 128)) if dg else ()),
                       writes=((pkey, off, off + w),), sem="st", inc=1)
                return pt_, pkey, off, w

            def emit_ac(g):
                def ac(e, g=g):
                    return e.tensor_add(st_col(C_LB0), st_col(C_LB0), st_col(C_LB0 + g))
                p.emit("vector", ac, reads=(st_r(C_LB0), st_r(C_LB0 + g)), writes=(st_r(C_LB0),),
                       sem="sv", inc=1)

            def emit_ex(g, with_ac=True):
                w = min(512, L - 512 * g)
                sti_, soff = blkpos(g)
                spt = P[sti_]
                spkey = f"P{sti_}"
                dst = C_LB0 + g

                def ex(e, g=g, w=w, spt=spt, soff=soff, dst=dst):
                    return e.activation(pbuf[:, 512 * g:512 * g + w], spt[:, soff:soff + w], AF.Exp,
                                        bias=st_col(C_M), scale=SCALE, accum_out=st_col(dst))
                p.emit("scalar", ex, reads=((spkey, soff, soff + w), st_r(C_M)),
                       writes=(("pbuf", 512 * g, 512 * g + w), st_r(dst)), sem="ss", inc=1)
                if with_ac and g > 0:
                    emit_ac(g)

            pt_d, pdkey, doff0, wd = emit_smm(bdiag)
            doff = doff0 + (L - 128 - 512 * bdiag)

            def rmd(e, pt_d=pt_d, doff=doff):
                return e.reduce_max(st_col(C_M), pt_d[:, doff:doff + 128], AX.X)
            p.emit("vector", rmd, reads=((pdkey, doff, doff + 128),), writes=(st_r(C_M),), sem="sv", inc=1)
            p.emit("vector", lambda e: e.tensor_scalar(st_col(C_M), st_col(C_M), -SCALE, None, ALU.mult),
                   reads=(st_r(C_M),), writes=(st_r(C_M),), sem="sv", inc=1)
            emit_ex(bdiag, with_ac=False)
            first_ex = None
            for b in range(min(nb, 6)):
                if b != bdiag:
                    emit_smm(b)
                    if first_ex is None:
                        emit_ex(b)
                        first_ex = b
            for b in range(min(nb, 6)):
                if b != bdiag and b != first_ex:
                    emit_ex(b)
            if bdiag > 0:
                emit_ac(bdiag)
            for b in (6, 7):
                if b < nb and b != bdiag:
                    emit_smm(b)

            def fmm(e, o=o):
                for c in range(8):
                    for h in range(2):
                        mm = e.matmul(P[2][:, 512 * h:512 * (h + 1)],
                                      xnst[:, 512 * c + 128 * o:512 * c + 128 * (o + 1)],
                                      Wl[:, 1024 * c + 512 * h:1024 * c + 512 * (h + 1)],
                                      start=(c == 0), stop=(c == 7))
                return mm
            xn_rd = tuple(("xnst", 512 * c + 128 * o, 512 * c + 128 * (o + 1)) for c in range(8))
            p.emit("tensor", fmm, reads=xn_rd + (("Wl", 0, 8192),),
                   writes=(("P2", 0, 1024),), sem="st", inc=1)

            def cmbA(e, ob=ob, xb=xb):
                return e.scalar_tensor_tensor(ob[:, :], P[2][:, :], st_col(C_ALPHA), xb[:, :], ALU.mult, ALU.add)
            p.emit("vector", cmbA, reads=(("P2", 0, 1024), st_r(C_ALPHA), (xkey, 0, D)),
                   writes=((okey, 0, D),), sem="sv", inc=1)
            if o == 3 and i + 1 < NT:
                def ldxn2(e, sti=sti):
                    return e.dma_start(xnst[:, :], xnspill_d[sti + 1, :, :])
                p.emit("sync", ldxn2, reads=(("xnspill", 4096 * (sti + 1), 4096 * (sti + 2)),),
                       writes=(("xnst", 0, 4096),), sem="qxn", inc=16)

            def emit_avm(g):
                j0, j1 = 4 * g, min(4 * g + 4, njt)
                off = 512 * (g % 3)

                def avm(e, j0=j0, j1=j1, off=off, i=i):
                    for j in range(j0, j1):
                        sl = pT[:, off + 128 * (j - j0):off + 128 * (j - j0 + 1)]
                        for h in range(2):
                            mm = e.matmul(P[1][:, 512 * h:512 * (h + 1)], sl,
                                          Vb[:, D * j + 512 * h:D * j + 512 * (h + 1)],
                                          start=(j == 0), stop=(j == i))
                    return mm
                p.emit("tensor", avm,
                       reads=(("pT", off, off + 128 * (j1 - j0)),) +
                             tuple(("Vb", D * j, D * (j + 1)) for j in range(j0, j1)),
                       writes=(("P1", 0, 1024),), sem="st", inc=1)

            for g in range(ng):
                j0, j1 = 4 * g, min(4 * g + 4, njt)
                off = 512 * (g % 2)
                toff = 512 * (g % 3)

                def trp(e, j0=j0, j1=j1, off=off):
                    for j in range(j0, j1):
                        mm = e.matmul(P[0][:, off + 128 * (j - j0):off + 128 * (j - j0 + 1)],
                                      pbuf[:, 128 * j:128 * (j + 1)], identb[:, :], start=True, stop=True)
                    return mm
                p.emit("tensor", trp, reads=(("pbuf", 128 * j0, 128 * j1), ("identb", 0, 128)),
                       writes=(("P0", off, off + 128 * (j1 - j0)),), sem="st", inc=1)

                if g + 4 < ng and g + 4 >= 6 and g + 4 != bdiag:
                    emit_ex(g + 4)

                if g % 2 == 0:
                    def evp(e, j0=j0, j1=j1, off=off, toff=toff):
                        return e.copy(pT[:, toff:toff + 128 * (j1 - j0)], P[0][:, off:off + 128 * (j1 - j0)])
                    p.emit("scalar", evp, reads=(("P0", off, off + 128 * (j1 - j0)),),
                           writes=(("pT", toff, toff + 128 * (j1 - j0)),), sem="ss", inc=1)
                else:
                    def evp(e, j0=j0, j1=j1, off=off, toff=toff):
                        return e.tensor_copy(pT[:, toff:toff + 128 * (j1 - j0)], P[0][:, off:off + 128 * (j1 - j0)])
                    p.emit("vector", evp, reads=(("P0", off, off + 128 * (j1 - j0)),),
                           writes=(("pT", toff, toff + 128 * (j1 - j0)),), sem="sv", inc=1)

                if g >= 2:
                    emit_avm(g - 2)
            if ng >= 2:
                emit_avm(ng - 2)
            emit_avm(ng - 1)
            p.emit("vector", lambda e: e.reciprocal(st_col(C_RL), st_col(C_LB0)),
                   reads=(st_r(C_LB0),), writes=(st_r(C_RL),), sem="sv", inc=1)

            def cmbB(e, ob=ob):
                return e.scalar_tensor_tensor(ob[:, :], P[1][:, :], st_col(C_RL), ob[:, :], ALU.mult, ALU.add)
            p.emit("vector", cmbB, reads=(("P1", 0, 1024), st_r(C_RL), (okey, 0, D)),
                   writes=((okey, 0, D),), sem="sv", inc=1)

            def cmb3(e, ob=ob):
                return e.tensor_add(ob[:, :], ob[:, :], biasbc[:, :])
            p.emit("gpsimd", cmb3, reads=((okey, 0, D), ("biasbc", 0, D)),
                   writes=((okey, 0, D),), sem="sg", inc=1)

            def sto(e, i=i, ob=ob):
                return e.dma_start(out_d[128 * i:128 * (i + 1), :], ob[:, :])
            p.emit("scalar", sto, reads=((okey, 0, D),), writes=(), sem=oq, inc=16)

        out_tot = {"qo0": p.cnt.get("qo0", 0), "qo1": p.cnt.get("qo1", 0)}

        def replay(e, ename):
            for (wl, fn, sem, inc) in p.ops[ename]:
                for (s2, v2) in wl:
                    e.wait_ge(sems[s2], v2)
                fn(e).then_inc(sems[sem], inc)

        @block.sync
        def _(e):
            replay(e, "sync")

        @block.tensor
        def _(e):
            replay(e, "tensor")

        @block.vector
        def _(e):
            replay(e, "vector")

        @block.scalar
        def _(e):
            from concourse.hw_specs import get_activation_tables
            for _idx, _funcs in enumerate(get_activation_tables(e.bass.m.arch).values()):
                if AF.Exp in _funcs and AF.Ln in _funcs and AF.Square in _funcs:
                    e.add_instruction(mybir.InstLoadActFuncSet(
                        name=e.bass.get_next_instruction_name(),
                        act_func_set_id=_idx, ins=[], outs=[]))
                    break
            replay(e, "scalar")
            e.wait_ge(sems["qo0"], out_tot["qo0"])
            e.wait_ge(sems["qo1"], out_tot["qo1"])

        @block.gpsimd
        def _(e):
            replay(e, "gpsimd")

    return nc


def kernel(L0_attn_x, L0_attn_WQ, L0_attn_WK, L0_attn_WV, L0_ffn_weight, L0_ffn_alpha, L0_ffn_bias):
    x = np.ascontiguousarray(np.asarray(L0_attn_x, dtype=np.float32))
    wqT = np.ascontiguousarray(np.asarray(L0_attn_WQ, dtype=np.float32).T)
    wkT = np.ascontiguousarray(np.asarray(L0_attn_WK, dtype=np.float32).T)
    wvT = np.ascontiguousarray(np.asarray(L0_attn_WV, dtype=np.float32).T)
    wfT = np.ascontiguousarray(np.asarray(L0_ffn_weight, dtype=np.float32).T)
    fb = np.ascontiguousarray(np.asarray(L0_ffn_bias, dtype=np.float32).reshape(1, -1))
    fa = np.ascontiguousarray(np.asarray(L0_ffn_alpha, dtype=np.float32).reshape(1, 1))

    nc = build_nc(4096)
    in_maps = []
    for b in range(NCORES):
        in_maps.append({
            "x": np.ascontiguousarray(x[b]),
            "wqT": wqT, "wkT": wkT, "wvT": wvT, "wfT": wfT,
            "fbias": fb, "falpha": fa,
        })
    res = run_bass_kernel_spmd(nc, in_maps, core_ids=list(range(NCORES)))
    out = np.stack([res.results[b]["out"] for b in range(NCORES)], axis=0)
    return out


# version 22
# speedup vs baseline: 1.0155x; 1.0155x over previous
import numpy as np
import ml_dtypes
from contextlib import ExitStack

import concourse.bass as bass
import concourse.mybir as mybir
from concourse.bass_utils import run_bass_kernel_spmd

F32 = mybir.dt.float32
BF16 = mybir.dt.bfloat16
AF = mybir.ActivationFunctionType
ALU = mybir.AluOpType
AX = mybir.AxisListType

D = 1024
NCORES = 8
EPS = 1e-5
SCALE = 1.0 / 32.0
NEG_BIG = -1.0e30

ENGINES = ("sync", "tensor", "vector", "scalar", "gpsimd")


class Plan:

    def __init__(self):
        self.ops = {e: [] for e in ENGINES}
        self.cnt = {}
        self.wm = {e: {} for e in ENGINES}
        self.wr = {}
        self.rd = {}

    @staticmethod
    def _expand(key, lo, hi):
        if key.startswith("P"):
            return (lo // 512) * 512, ((hi + 511) // 512) * 512
        return lo, hi

    def emit(self, eng, fn, reads=(), writes=(), sem=None, inc=0, waits=(),
             marker_override=None):
        need = {}
        for (k, lo, hi) in reads:
            lo, hi = self._expand(k, lo, hi)
            for (l2, h2, s2, v2) in self.wr.get(k, ()):
                if l2 < hi and lo < h2:
                    need[s2] = max(need.get(s2, 0), v2)
        for (k, lo, hi) in writes:
            lo, hi = self._expand(k, lo, hi)
            for src in (self.wr.get(k, ()), self.rd.get(k, ())):
                for (l2, h2, s2, v2) in src:
                    if l2 < hi and lo < h2:
                        need[s2] = max(need.get(s2, 0), v2)
        for (s2, v2) in waits:
            need[s2] = max(need.get(s2, 0), v2)

        wmap = self.wm[eng]
        wl = []
        for s2, v2 in sorted(need.items()):
            if wmap.get(s2, 0) < v2:
                wl.append((s2, v2))
                wmap[s2] = v2
        assert inc, "every op must carry an inc"
        self.cnt[sem] = self.cnt.get(sem, 0) + inc
        mval = self.cnt[sem] if marker_override is None else marker_override
        marker = (sem, mval)

        self.ops[eng].append((wl, fn, sem, inc))

        for (k, lo, hi) in writes:
            lo, hi = self._expand(k, lo, hi)
            lst = [e for e in self.wr.get(k, ()) if not (e[0] >= lo and e[1] <= hi)]
            lst.append((lo, hi, marker[0], marker[1]))
            self.wr[k] = lst
            self.rd[k] = [e for e in self.rd.get(k, ()) if not (e[0] >= lo and e[1] <= hi)]
        for (k, lo, hi) in reads:
            lo, hi = self._expand(k, lo, hi)
            self.rd.setdefault(k, []).append((lo, hi, marker[0], marker[1]))


def build_nc(S=4096):
    NT = S // 128
    NST = S // 512
    PB = max(S, 4096)
    nc = bass.Bass("TRN2", target_bir_lowering=False)

    x_d = nc.dram_tensor("x", [S, D], F32, kind="ExternalInput").ap()
    wqT_d = nc.dram_tensor("wqT", [D, D], F32, kind="ExternalInput").ap()
    wkT_d = nc.dram_tensor("wkT", [D, D], F32, kind="ExternalInput").ap()
    wvT_d = nc.dram_tensor("wvT", [D, D], F32, kind="ExternalInput").ap()
    wfT_d = nc.dram_tensor("wfT", [D, D], F32, kind="ExternalInput").ap()
    bias_d = nc.dram_tensor("fbias", [1, D], F32, kind="ExternalInput").ap()
    alpha_d = nc.dram_tensor("falpha", [1, 1], F32, kind="ExternalInput").ap()
    out_d = nc.dram_tensor("out", [S, D], F32, kind="ExternalOutput").ap()

    xnspill_d = nc.dram_tensor("xnspill", [NST, 128, 8 * 512], BF16, kind="Internal").ap()
    wftspill_d = nc.dram_tensor("wftspill", [128, 8 * D], BF16, kind="Internal").ap()

    ident_h = nc.inline_tensor(np.eye(128, dtype=ml_dtypes.bfloat16), "identc").ap()
    mbv = np.where(np.arange(128)[None, :] <= np.arange(128)[:, None], 0.0, NEG_BIG)
    mask_h = nc.inline_tensor(mbv.astype(ml_dtypes.bfloat16), "maskc").ap()
    onesc_h = nc.inline_tensor(np.ones((128, 1), dtype=np.float32), "onescc").ap()
    onesr_h = nc.inline_tensor(np.ones((1, 128), dtype=np.float32), "onesrc").ap()
    onesrb_h = nc.inline_tensor(np.ones((1, 128), dtype=ml_dtypes.bfloat16), "onesrbc").ap()

    p = Plan()

    with ExitStack() as es:
        def sb(name, shape, dtype):
            return es.enter_context(nc.sbuf_tensor(name, shape, dtype))

        Kb = sb("Kb", [128, 8 * S], BF16)
        Vb = sb("Vb", [128, NT * D], BF16)
        Wl = sb("Wl", [128, 2 * 8 * D], BF16)
        xnst = sb("xnst", [128, 8 * 512], BF16)
        qst = sb("qst", [128, 8 * 512], BF16)
        pbuf = sb("pbuf", [128, PB], BF16)
        pT = sb("pT", [128, 1536], BF16)
        xin = [sb("xin0", [128, D], F32), sb("xin1", [128, D], F32)]
        outb = [sb("outb0", [128, D], F32), sb("outb1", [128, D], F32)]
        biasbc = sb("biasbc", [128, D], BF16)
        maskb = sb("maskb", [128, 128], BF16)
        identb = sb("identb", [128, 128], BF16)
        onesc = sb("onesc", [128, 1], F32)
        onesr = sb("onesr", [1, 128], F32)
        onesrb = sb("onesrb", [1, 128], BF16)
        stats = sb("stats", [128, 32], F32)
        dsc = sb("dsc", [1, 1], F32)
        alsc = sb("alsc", [1, 1], F32)

        P = [es.enter_context(nc.psum_tensor(f"P{i}", [128, 1024], F32)) for i in range(4)]

        semnames = ("qc", "qbc", "qw0", "qw1", "qwv", "qwv2", "qf0", "qf1", "qb1", "qb2", "qx0", "qx1", "qxn",
                    "qsp", "qo0", "qo1", "st", "sv", "ss", "sg")
        sems = {n: es.enter_context(nc.semaphore(n)) for n in semnames}
        block = es.enter_context(nc.Block())

        C_ABS, C_ABS2 = 0, 1
        C_DELTA, C_NDELTA, C_ALPHA = 2, 3, 4
        LN_M = (5, 10)
        LN_SQ = (6, 11)
        LN_V = (7, 12)
        LN_R = (8, 13)
        C_M = 9
        C_MB0 = 14
        C_LB0 = 20
        C_RL = 28
        C_EPS = 29
        C_MX0 = 14

        def st_col(c):
            return stats[:, c:c + 1]

        def st_r(c):
            return ("stats", c, c + 1)

        p.emit("vector", lambda e: e.memset(st_col(C_EPS), EPS),
               writes=(st_r(C_EPS),), sem="sv", inc=1)
        for t0_ in (0, 1):
            def ldx0(e, t0_=t0_):
                return e.dma_start(xin[t0_][:, :], x_d[128 * t0_:128 * (t0_ + 1), :])
            p.emit("sync", ldx0, writes=((f"xin{t0_}", 0, D),), sem=f"qx{t0_}", inc=16)
        for c in range(8):
            def ldv(e, c=c):
                return e.dma_start(Wl[:, 8192 + 1024 * c:8192 + 1024 * (c + 1)], wvT_d[128 * c:128 * (c + 1), :])
            p.emit("gpsimd", ldv, writes=(("Wl", 8192 + 1024 * c, 8192 + 1024 * (c + 1)),),
                   sem="qwv2", inc=16, marker_override=128)
        for c in range(8):
            def ldk(e, c=c):
                return e.dma_start(Wl[:, 1024 * c:1024 * (c + 1)], wkT_d[128 * c:128 * (c + 1), :])
            p.emit("gpsimd", ldk, writes=(("Wl", 1024 * c, 1024 * (c + 1)),),
                   sem="qwv", inc=16, marker_override=128)
        assert p.cnt["qwv"] == 128 and p.cnt["qwv2"] == 128

        NCONST = 6
        p.emit("sync", lambda e: e.dma_start(identb[:, :], ident_h[:, :]),
               writes=(("identb", 0, 128),), sem="qc", inc=16, marker_override=16 * NCONST)
        p.emit("sync", lambda e: e.dma_start(maskb[:, :], mask_h[:, :]),
               writes=(("maskb", 0, 128),), sem="qc", inc=16, marker_override=16 * NCONST)
        p.emit("sync", lambda e: e.dma_start(onesc[:, :], onesc_h[:, :]),
               writes=(("onesc", 0, 1),), sem="qc", inc=16, marker_override=16 * NCONST)
        p.emit("sync", lambda e: e.dma_start(onesr[:, :], onesr_h[:, :]),
               writes=(("onesr", 0, 128),), sem="qc", inc=16, marker_override=16 * NCONST)
        p.emit("sync", lambda e: e.dma_start(onesrb[:, :], onesrb_h[:, :]),
               writes=(("onesrb", 0, 128),), sem="qc", inc=16, marker_override=16 * NCONST)
        p.emit("sync", lambda e: e.dma_start(alsc[:, :], alpha_d[:, :]),
               writes=(("alsc", 0, 1),), sem="qc", inc=16, marker_override=16 * NCONST)
        p.emit("gpsimd", lambda e: e.dma_start(biasbc[:1, :], bias_d[:, :]),
               writes=(("biasbc", 0, D),), sem="qbc", inc=16)
        assert p.cnt["qc"] == 16 * NCONST

        def emit_tern_p1(cs):
            for c in cs:
                ob = outb[c % 2]
                okey = f"outb{c % 2}"

                def ld(e, c=c, ob=ob):
                    return e.dma_start(ob[:, :], wfT_d[128 * c:128 * (c + 1), :])
                p.emit("sync", ld, writes=((okey, 0, D),), sem=f"qw{c % 2}", inc=16)
                dst = C_ABS if c == 0 else C_ABS2

                def red(e, dst=dst, ob=ob):
                    return e.reduce_sum(st_col(dst), ob[:, :], AX.X, apply_absolute_value=True)
                p.emit("vector", red, reads=((okey, 0, D),), writes=(st_r(dst),), sem="sv", inc=1)
                if c > 0:
                    p.emit("vector", lambda e: e.tensor_add(st_col(C_ABS), st_col(C_ABS), st_col(C_ABS2)),
                           reads=(st_r(C_ABS), st_r(C_ABS2)), writes=(st_r(C_ABS),), sem="sv", inc=1)

        def emit_tern_mid():
            p.emit("tensor", lambda e: e.matmul(P[0][:1, 0:1], onesc[:, :], st_col(C_ABS), start=True, stop=True),
                   reads=(("onesc", 0, 1), st_r(C_ABS)), writes=(("P0", 0, 1),), sem="st", inc=1)
            p.emit("vector", lambda e: e.tensor_copy(dsc[:, :], P[0][:1, 0:1]),
                   reads=(("P0", 0, 1),), writes=(("dsc", 0, 1),), sem="sv", inc=1)
            p.emit("tensor", lambda e: e.matmul(P[0][:, 0:1], onesr[:, :], dsc[:, :], start=True, stop=True),
                   reads=(("onesr", 0, 128), ("dsc", 0, 1)), writes=(("P0", 0, 1),), sem="st", inc=1)
            p.emit("vector", lambda e: e.tensor_scalar(st_col(C_DELTA), P[0][:, 0:1], 0.7 / (1024.0 * 1024.0), None, ALU.mult),
                   reads=(("P0", 0, 1),), writes=(st_r(C_DELTA),), sem="sv", inc=1)
            p.emit("vector", lambda e: e.tensor_scalar(st_col(C_NDELTA), st_col(C_DELTA), -1.0, None, ALU.mult),
                   reads=(st_r(C_DELTA),), writes=(st_r(C_NDELTA),), sem="sv", inc=1)
            p.emit("tensor", lambda e: e.matmul(P[0][:, 0:1], onesr[:, :], alsc[:, :], start=True, stop=True),
                   reads=(("onesr", 0, 128), ("alsc", 0, 1)), writes=(("P0", 0, 1),), sem="st", inc=1)
            p.emit("vector", lambda e: e.tensor_copy(st_col(C_ALPHA), P[0][:, 0:1]),
                   reads=(("P0", 0, 1),), writes=(st_r(C_ALPHA),), sem="sv", inc=1)
            for h in range(2):
                def bmm(e, h=h):
                    return e.matmul(P[1][:, 512 * h:512 * (h + 1)], onesrb[:, :],
                                    biasbc[:1, 512 * h:512 * (h + 1)], start=True, stop=True)
                p.emit("tensor", bmm, reads=(("onesrb", 0, 128), ("biasbc", 512 * h, 512 * h + 512)),
                       writes=(("P1", 512 * h, 512 * h + 512),), sem="st", inc=1)
            p.emit("vector", lambda e: e.tensor_copy(biasbc[:, :], P[1][:, :]),
                   reads=(("P1", 0, 1024),), writes=(("biasbc", 0, D),), sem="sv", inc=1)

        def emit_tern_p2(cs):
            for c in cs:
                ob = outb[c % 2]
                okey = f"outb{c % 2}"
                so_lo = 1024 + 1024 * (c % 2)

                def ld2(e, c=c, ob=ob):
                    return e.dma_start(ob[:, :], wfT_d[128 * c:128 * (c + 1), :])
                p.emit("sync", ld2, writes=((okey, 0, D),), sem=f"qw{c % 2}", inc=16)
                p.emit("vector", lambda e, ob=ob: e.tensor_scalar(pbuf[:, 2048:3072], ob[:, :], st_col(C_DELTA), None, ALU.is_gt),
                       reads=((okey, 0, D), st_r(C_DELTA)), writes=(("pbuf", 2048, 3072),), sem="sv", inc=1)
                p.emit("vector", lambda e, ob=ob: e.tensor_scalar(pbuf[:, 3072:4096], ob[:, :], st_col(C_NDELTA), None, ALU.is_lt),
                       reads=((okey, 0, D), st_r(C_NDELTA)), writes=(("pbuf", 3072, 4096),), sem="sv", inc=1)

                def sub(e, so_lo=so_lo):
                    return e.tensor_sub(qst[:, so_lo:so_lo + 1024], pbuf[:, 2048:3072], pbuf[:, 3072:4096])
                p.emit("vector", sub, reads=(("pbuf", 2048, 4096),), writes=(("qst", so_lo, so_lo + 1024),),
                       sem="sv", inc=1)

                def spw(e, c=c, so_lo=so_lo):
                    return e.dma_start(wftspill_d[:, 1024 * c:1024 * (c + 1)], qst[:, so_lo:so_lo + 1024])
                p.emit("gpsimd", spw, reads=(("qst", so_lo, so_lo + 1024),),
                       writes=(("wftspill", 1024 * c, 1024 * (c + 1)),), sem=f"qf{c % 2}", inc=16)

        xnst_v = xnst[:].rearrange("p (c t) -> p c t", c=8)

        def emit_lnA(t):
            par = t % 2
            xb = xin[par]
            xkey = f"xin{par}"
            xq = f"qx{par}"
            cm, cs, cv, cr = LN_M[par], LN_SQ[par], LN_V[par], LN_R[par]

            if t >= 2:
                def ldx(e, t=t, xb=xb):
                    return e.dma_start(xb[:, :], x_d[128 * t:128 * (t + 1), :])
                p.emit("sync", ldx, writes=((xkey, 0, D),), sem=xq, inc=16)

            def r1(e, xb=xb, cm=cm):
                return e.activation(qst[:, 0:D], xb[:, :], AF.Copy, accum_out=st_col(cm))
            p.emit("scalar", r1, reads=((xkey, 0, D),), writes=(("qst", 0, D), st_r(cm)), sem="ss", inc=1)
            p.emit("scalar", lambda e, cm=cm: e.mul(st_col(cm), st_col(cm), -1.0 / D),
                   reads=(st_r(cm),), writes=(st_r(cm),), sem="ss", inc=1)

            def sq(e, xb=xb, cm=cm, cs=cs):
                return e.activation(qst[:, 0:D], xb[:, :], AF.Square, bias=st_col(cm),
                                    scale=1.0, accum_out=st_col(cs))
            p.emit("scalar", sq, reads=((xkey, 0, D), st_r(cm)),
                   writes=(("qst", 0, D), st_r(cs)), sem="ss", inc=1)
            p.emit("scalar", lambda e, cs=cs, cv=cv: e.activation(st_col(cv), st_col(cs), AF.Ln,
                                                                  bias=st_col(C_EPS), scale=1.0 / D),
                   reads=(st_r(cs), st_r(C_EPS)), writes=(st_r(cv),), sem="ss", inc=1)
            p.emit("scalar", lambda e, cv=cv, cr=cr: e.activation(st_col(cr), st_col(cv), AF.Exp, bias=0.0, scale=-0.5),
                   reads=(st_r(cv),), writes=(st_r(cr),), sem="ss", inc=1)

        def emit_lnB(t):
            par = t % 2
            xb = xin[par]
            xkey = f"xin{par}"
            cm, cr = LN_M[par], LN_R[par]
            xtm_lo = 1024 * par

            def norm(e, xb=xb, cm=cm, cr=cr, xtm_lo=xtm_lo):
                return e.tensor_scalar(pbuf[:, xtm_lo:xtm_lo + D], xb[:, :], st_col(cm), st_col(cr), ALU.add, ALU.mult)
            p.emit("vector", norm, reads=((xkey, 0, D), st_r(cm), st_r(cr)),
                   writes=(("pbuf", xtm_lo, xtm_lo + D),), sem="sv", inc=1)
            return xtm_lo

        def emit_trs(t, tt, xtm_lo):
            par = t % 2
            pd = P[par]
            pdkey = f"P{par}"

            def trs(e, xtm_lo=xtm_lo, pd=pd):
                for c in range(8):
                    mm = e.matmul(pd[:, 128 * c:128 * (c + 1)], pbuf[:, xtm_lo + 128 * c:xtm_lo + 128 * (c + 1)],
                                  identb[:, :], start=True, stop=True)
                return mm
            p.emit("tensor", trs, reads=(("pbuf", xtm_lo, xtm_lo + D), ("identb", 0, 128)),
                   writes=((pdkey, 0, 1024),), sem="st", inc=1)

            xn_wr = tuple(("xnst", 512 * c + 128 * tt, 512 * c + 128 * (tt + 1)) for c in range(8))

            def evx(e, tt=tt, pd=pd):
                return e.tensor_copy(xnst_v[:, :, 128 * tt:128 * (tt + 1)],
                                     pd[:].rearrange("p (c t) -> p c t", c=8)[:, :, :])
            p.emit("vector", evx, reads=((pdkey, 0, 1024),), writes=xn_wr, sem="sv", inc=1)
            return xn_wr

        def emit_v(t, tt, xn_wr):
            par = t % 2
            pv = P[2 + par]
            pvkey = f"P{2 + par}"

            def vmm(e, tt=tt, pv=pv):
                for c in range(8):
                    for h in range(2):
                        mm = e.matmul(pv[:, 512 * h:512 * (h + 1)],
                                      xnst[:, 512 * c + 128 * tt:512 * c + 128 * (tt + 1)],
                                      Wl[:, 8192 + 1024 * c + 512 * h:8192 + 1024 * c + 512 * (h + 1)],
                                      start=(c == 0), stop=(c == 7))
                return mm
            p.emit("tensor", vmm, reads=xn_wr + (("Wl", 8192, 16384),),
                   writes=((pvkey, 0, 1024),), sem="st", inc=1)

            def evv(e, t=t, pv=pv):
                return e.copy(Vb[:, D * t:D * (t + 1)], pv[:, :])
            p.emit("scalar", evv, reads=((pvkey, 0, 1024),), writes=(("Vb", D * t, D * (t + 1)),),
                   sem="ss", inc=1)

        lnA_done = set()
        lnB_done = set()
        carryV = []
        for sti in range(NST):
            tiles = [4 * sti + k for k in range(4)]
            Bq = [t for t in tiles if t not in lnB_done]
            Vq = list(carryV)
            carryV = []
            for tt in range(4):
                t = tiles[tt]
                if t not in lnA_done:
                    emit_lnA(t)
                    lnA_done.add(t)
                if tt >= 1 and (len(Vq) >= 2 or (tt >= 2 and Vq)):
                    emit_v(*Vq.pop(0))
                if tt >= 1 and Bq:
                    tb = Bq.pop(0)
                    xtm_lo = emit_lnB(tb)
                    xn_wr = emit_trs(tb, tb % 4, xtm_lo)
                    Vq.append((tb, tb % 4, xn_wr))
                    lnB_done.add(tb)
                    if tb + 2 < 4 * NST and tb + 2 not in lnA_done:
                        emit_lnA(tb + 2)
                        lnA_done.add(tb + 2)
            while Bq:
                tb = Bq.pop(0)
                xtm_lo = emit_lnB(tb)
                xn_wr = emit_trs(tb, tb % 4, xtm_lo)
                Vq.append((tb, tb % 4, xn_wr))
                lnB_done.add(tb)
                if tb + 2 < 4 * NST and tb + 2 not in lnA_done:
                    emit_lnA(tb + 2)
                    lnA_done.add(tb + 2)
            while Vq:
                emit_v(*Vq.pop(0))

            def spl(e, sti=sti):
                return e.dma_start(xnspill_d[sti, :, :], xnst[:, :])
            p.emit("gpsimd", spl, reads=(("xnst", 0, 4096),),
                   writes=(("xnspill", 4096 * sti, 4096 * (sti + 1)),), sem="qsp", inc=16)

            if NST >= 8:
                if sti == 0:
                    emit_tern_p1(range(0, 4))
                elif sti == 1:
                    emit_tern_p1(range(4, 8))
                elif sti == 2:
                    emit_tern_mid()
                elif sti in (3, 4, 5, 6):
                    emit_tern_p2(range(2 * (sti - 3), 2 * (sti - 2)))
            else:
                if sti == NST - 1:
                    emit_tern_p1(range(0, 8))
                    emit_tern_mid()
                    emit_tern_p2(range(0, 8))

            for kd in range(8):
                bank = kd % 4
                pt_ = P[bank // 2]
                pkey = f"P{bank // 2}"
                off = 512 * (bank % 2)

                def kmm(e, kd=kd, pt_=pt_, off=off):
                    for c in range(8):
                        mm = e.matmul(pt_[:, off:off + 512],
                                      Wl[:, 1024 * c + 128 * kd:1024 * c + 128 * (kd + 1)],
                                      xnst[:, 512 * c:512 * (c + 1)],
                                      start=(c == 0), stop=(c == 7))
                    return mm
                p.emit("tensor", kmm, reads=(("xnst", 0, 4096), ("Wl", 0, 8192)),
                       writes=((pkey, off, off + 512),), sem="st", inc=1)

                def evk(e, kd=kd, sti=sti, pt_=pt_, off=off):
                    return e.copy(Kb[:, S * kd + 512 * sti:S * kd + 512 * (sti + 1)], pt_[:, off:off + 512])
                p.emit("scalar", evk, reads=((pkey, off, off + 512),),
                       writes=(("Kb", S * kd + 512 * sti, S * kd + 512 * (sti + 1)),), sem="ss", inc=1)

            if sti + 1 < NST:
                tn = 4 * sti + 4
                xtm_lo = emit_lnB(tn)
                xn_wr = emit_trs(tn, 0, xtm_lo)
                carryV.append((tn, 0, xn_wr))
                lnB_done.add(tn)
                if tn + 2 < 4 * NST and tn + 2 not in lnA_done:
                    emit_lnA(tn + 2)
                    lnA_done.add(tn + 2)


        for c in range(8):
            def ldq(e, c=c):
                return e.dma_start(Wl[:, 8192 + 1024 * c:8192 + 1024 * (c + 1)], wqT_d[128 * c:128 * (c + 1), :])
            p.emit("gpsimd", ldq, writes=(("Wl", 8192 + 1024 * c, 8192 + 1024 * (c + 1)),),
                   sem="qb2", inc=16, marker_override=128)
        assert p.cnt["qb2"] == 128
        p.emit("gpsimd", lambda e: e.dma_start(Wl[:, 0:8192], wftspill_d[:, :]),
               reads=(("wftspill", 0, 8192),), writes=(("Wl", 0, 8192),), sem="qb1", inc=16)

        for i in range(NT):
            sti, o = i // 4, i % 4
            L = 128 * (i + 1)
            nb = (L + 511) // 512

            if o == 0:
                if i == 0:
                    def ldxn(e, sti=sti):
                        return e.dma_start(xnst[:, :], xnspill_d[sti, :, :])
                    p.emit("sync", ldxn, reads=(("xnspill", 4096 * sti, 4096 * (sti + 1)),),
                           writes=(("xnst", 0, 4096),), sem="qxn", inc=16)
                for kd in range(8):
                    off = 512 * (kd % 2)

                    def qmm(e, kd=kd, off=off):
                        for c in range(8):
                            mm = e.matmul(P[3][:, off:off + 512],
                                          Wl[:, 8192 + 1024 * c + 128 * kd:8192 + 1024 * c + 128 * (kd + 1)],
                                          xnst[:, 512 * c:512 * (c + 1)],
                                          start=(c == 0), stop=(c == 7))
                        return mm
                    p.emit("tensor", qmm, reads=(("xnst", 0, 4096), ("Wl", 8192, 16384)),
                           writes=(("P3", off, off + 512),), sem="st", inc=1)

                    def evq(e, kd=kd, off=off):
                        return e.tensor_copy(qst[:, 512 * kd:512 * (kd + 1)], P[3][:, off:off + 512])
                    p.emit("vector", evq, reads=(("P3", off, off + 512),),
                           writes=(("qst", 512 * kd, 512 * (kd + 1)),), sem="sv", inc=1)

            xb = xin[i % 2]
            xkey = f"xin{i % 2}"
            xq = f"qx{i % 2}"
            ob = outb[i % 2]
            okey = f"outb{i % 2}"
            oq = f"qo{i % 2}"

            def ldx2(e, i=i, xb=xb):
                return e.dma_start(xb[:, :], x_d[128 * i:128 * (i + 1), :])
            p.emit("sync", ldx2, writes=((xkey, 0, D),), sem=xq, inc=16)

            q_rd = tuple(("qst", 512 * kd + 128 * o, 512 * kd + 128 * (o + 1)) for kd in range(8))
            bdiag = (L - 128) // 512
            njt = i + 1
            ng = (njt + 3) // 4
            assert ng == nb

            def blkpos(b):
                if b >= 4:
                    return 3, 512 * ((b - 4) % 2)
                if b == bdiag:
                    return 3, 512 * (b % 2)
                return b // 2, 512 * (b % 2)

            def emit_smm(b):
                w = min(512, L - 512 * b)
                ti, off = blkpos(b)
                pt_ = P[ti]
                pkey = f"P{ti}"
                dg = (b == bdiag)
                dof = off + (L - 128 - 512 * b) if dg else 0

                def smm(e, b=b, w=w, pt_=pt_, off=off, o=o, dg=dg, dof=dof):
                    for kd in range(8):
                        mm = e.matmul(pt_[:, off:off + w],
                                      qst[:, 512 * kd + 128 * o:512 * kd + 128 * (o + 1)],
                                      Kb[:, S * kd + 512 * b:S * kd + 512 * b + w],
                                      start=(kd == 0), stop=(kd == 7 and not dg))
                    if dg:
                        mm = e.matmul(pt_[:, dof:dof + 128], identb[:, :], maskb[:, :],
                                      start=False, stop=True)
                    return mm
                p.emit("tensor", smm,
                       reads=q_rd + tuple(("Kb", S * kd + 512 * b, S * kd + 512 * b + w) for kd in range(8))
                             + ((("identb", 0, 128), ("maskb", 0, 128)) if dg else ()),
                       writes=((pkey, off, off + w),), sem="st", inc=1)
                return pt_, pkey, off, w

            def emit_ac(g):
                def ac(e, g=g):
                    return e.tensor_add(st_col(C_LB0), st_col(C_LB0), st_col(C_LB0 + g))
                p.emit("vector", ac, reads=(st_r(C_LB0), st_r(C_LB0 + g)), writes=(st_r(C_LB0),),
                       sem="sv", inc=1)

            def emit_ex(g, with_ac=True):
                w = min(512, L - 512 * g)
                sti_, soff = blkpos(g)
                spt = P[sti_]
                spkey = f"P{sti_}"
                dst = C_LB0 + g

                def ex(e, g=g, w=w, spt=spt, soff=soff, dst=dst):
                    return e.activation(pbuf[:, 512 * g:512 * g + w], spt[:, soff:soff + w], AF.Exp,
                                        bias=st_col(C_M), scale=SCALE, accum_out=st_col(dst))
                p.emit("scalar", ex, reads=((spkey, soff, soff + w), st_r(C_M)),
                       writes=(("pbuf", 512 * g, 512 * g + w), st_r(dst)), sem="ss", inc=1)
                if with_ac and g > 0:
                    emit_ac(g)

            pt_d, pdkey, doff0, wd = emit_smm(bdiag)
            doff = doff0 + (L - 128 - 512 * bdiag)

            def rmd(e, pt_d=pt_d, doff=doff):
                return e.reduce_max(st_col(C_M), pt_d[:, doff:doff + 128], AX.X)
            p.emit("vector", rmd, reads=((pdkey, doff, doff + 128),), writes=(st_r(C_M),), sem="sv", inc=1)
            p.emit("vector", lambda e: e.tensor_scalar(st_col(C_M), st_col(C_M), -SCALE, None, ALU.mult),
                   reads=(st_r(C_M),), writes=(st_r(C_M),), sem="sv", inc=1)
            emit_ex(bdiag, with_ac=False)
            first_ex = None
            for b in range(min(nb, 6)):
                if b != bdiag:
                    emit_smm(b)
                    if first_ex is None:
                        emit_ex(b)
                        first_ex = b
            for b in range(min(nb, 6)):
                if b != bdiag and b != first_ex:
                    emit_ex(b)
            if bdiag > 0:
                emit_ac(bdiag)
            for b in (6, 7):
                if b < nb and b != bdiag:
                    emit_smm(b)

            def fmm(e, o=o):
                for c in range(8):
                    for h in range(2):
                        mm = e.matmul(P[2][:, 512 * h:512 * (h + 1)],
                                      xnst[:, 512 * c + 128 * o:512 * c + 128 * (o + 1)],
                                      Wl[:, 1024 * c + 512 * h:1024 * c + 512 * (h + 1)],
                                      start=(c == 0), stop=(c == 7))
                return mm
            xn_rd = tuple(("xnst", 512 * c + 128 * o, 512 * c + 128 * (o + 1)) for c in range(8))
            p.emit("tensor", fmm, reads=xn_rd + (("Wl", 0, 8192),),
                   writes=(("P2", 0, 1024),), sem="st", inc=1)

            def cmbA(e, ob=ob, xb=xb):
                return e.scalar_tensor_tensor(ob[:, :], P[2][:, :], st_col(C_ALPHA), xb[:, :], ALU.mult, ALU.add)
            p.emit("vector", cmbA, reads=(("P2", 0, 1024), st_r(C_ALPHA), (xkey, 0, D)),
                   writes=((okey, 0, D),), sem="sv", inc=1)
            if o == 3 and i + 1 < NT:
                def ldxn2(e, sti=sti):
                    return e.dma_start(xnst[:, :], xnspill_d[sti + 1, :, :])
                p.emit("sync", ldxn2, reads=(("xnspill", 4096 * (sti + 1), 4096 * (sti + 2)),),
                       writes=(("xnst", 0, 4096),), sem="qxn", inc=16)

            def emit_avm(g):
                j0, j1 = 4 * g, min(4 * g + 4, njt)
                off = 512 * (g % 3)

                def avm(e, j0=j0, j1=j1, off=off, i=i):
                    for j in range(j0, j1):
                        sl = pT[:, off + 128 * (j - j0):off + 128 * (j - j0 + 1)]
                        for h in range(2):
                            mm = e.matmul(P[1][:, 512 * h:512 * (h + 1)], sl,
                                          Vb[:, D * j + 512 * h:D * j + 512 * (h + 1)],
                                          start=(j == 0), stop=(j == i))
                    return mm
                p.emit("tensor", avm,
                       reads=(("pT", off, off + 128 * (j1 - j0)),) +
                             tuple(("Vb", D * j, D * (j + 1)) for j in range(j0, j1)),
                       writes=(("P1", 0, 1024),), sem="st", inc=1)

            for g in range(ng):
                j0, j1 = 4 * g, min(4 * g + 4, njt)
                off = 512 * (g % 2)
                toff = 512 * (g % 3)

                def trp(e, j0=j0, j1=j1, off=off):
                    for j in range(j0, j1):
                        mm = e.matmul(P[0][:, off + 128 * (j - j0):off + 128 * (j - j0 + 1)],
                                      pbuf[:, 128 * j:128 * (j + 1)], identb[:, :], start=True, stop=True)
                    return mm
                p.emit("tensor", trp, reads=(("pbuf", 128 * j0, 128 * j1), ("identb", 0, 128)),
                       writes=(("P0", off, off + 128 * (j1 - j0)),), sem="st", inc=1)

                if g + 4 < ng and g + 4 >= 6 and g + 4 != bdiag:
                    emit_ex(g + 4)

                def evp(e, j0=j0, j1=j1, off=off, toff=toff):
                    return e.copy(pT[:, toff:toff + 128 * (j1 - j0)], P[0][:, off:off + 128 * (j1 - j0)])
                p.emit("scalar", evp, reads=(("P0", off, off + 128 * (j1 - j0)),),
                       writes=(("pT", toff, toff + 128 * (j1 - j0)),), sem="ss", inc=1)

                if g >= 2:
                    emit_avm(g - 2)
            if ng >= 2:
                emit_avm(ng - 2)
            emit_avm(ng - 1)
            p.emit("vector", lambda e: e.reciprocal(st_col(C_RL), st_col(C_LB0)),
                   reads=(st_r(C_LB0),), writes=(st_r(C_RL),), sem="sv", inc=1)

            def cmbB(e, ob=ob):
                return e.scalar_tensor_tensor(ob[:, :], P[1][:, :], st_col(C_RL), ob[:, :], ALU.mult, ALU.add)
            p.emit("vector", cmbB, reads=(("P1", 0, 1024), st_r(C_RL), (okey, 0, D)),
                   writes=((okey, 0, D),), sem="sv", inc=1)

            def cmb3(e, ob=ob):
                return e.tensor_add(ob[:, :], ob[:, :], biasbc[:, :])
            p.emit("gpsimd", cmb3, reads=((okey, 0, D), ("biasbc", 0, D)),
                   writes=((okey, 0, D),), sem="sg", inc=1)

            def sto(e, i=i, ob=ob):
                return e.dma_start(out_d[128 * i:128 * (i + 1), :], ob[:, :])
            p.emit("scalar", sto, reads=((okey, 0, D),), writes=(), sem=oq, inc=16)

        out_tot = {"qo0": p.cnt.get("qo0", 0), "qo1": p.cnt.get("qo1", 0)}

        def replay(e, ename):
            for (wl, fn, sem, inc) in p.ops[ename]:
                for (s2, v2) in wl:
                    e.wait_ge(sems[s2], v2)
                fn(e).then_inc(sems[sem], inc)

        @block.sync
        def _(e):
            replay(e, "sync")

        @block.tensor
        def _(e):
            replay(e, "tensor")

        @block.vector
        def _(e):
            replay(e, "vector")

        @block.scalar
        def _(e):
            from concourse.hw_specs import get_activation_tables
            for _idx, _funcs in enumerate(get_activation_tables(e.bass.m.arch).values()):
                if AF.Exp in _funcs and AF.Ln in _funcs and AF.Square in _funcs:
                    e.add_instruction(mybir.InstLoadActFuncSet(
                        name=e.bass.get_next_instruction_name(),
                        act_func_set_id=_idx, ins=[], outs=[]))
                    break
            replay(e, "scalar")
            e.wait_ge(sems["qo0"], out_tot["qo0"])
            e.wait_ge(sems["qo1"], out_tot["qo1"])

        @block.gpsimd
        def _(e):
            replay(e, "gpsimd")

    return nc


def kernel(L0_attn_x, L0_attn_WQ, L0_attn_WK, L0_attn_WV, L0_ffn_weight, L0_ffn_alpha, L0_ffn_bias):
    x = np.ascontiguousarray(np.asarray(L0_attn_x, dtype=np.float32))
    wqT = np.ascontiguousarray(np.asarray(L0_attn_WQ, dtype=np.float32).T)
    wkT = np.ascontiguousarray(np.asarray(L0_attn_WK, dtype=np.float32).T)
    wvT = np.ascontiguousarray(np.asarray(L0_attn_WV, dtype=np.float32).T)
    wfT = np.ascontiguousarray(np.asarray(L0_ffn_weight, dtype=np.float32).T)
    fb = np.ascontiguousarray(np.asarray(L0_ffn_bias, dtype=np.float32).reshape(1, -1))
    fa = np.ascontiguousarray(np.asarray(L0_ffn_alpha, dtype=np.float32).reshape(1, 1))

    nc = build_nc(4096)
    in_maps = []
    for b in range(NCORES):
        in_maps.append({
            "x": np.ascontiguousarray(x[b]),
            "wqT": wqT, "wkT": wkT, "wvT": wvT, "wfT": wfT,
            "fbias": fb, "falpha": fa,
        })
    res = run_bass_kernel_spmd(nc, in_maps, core_ids=list(range(NCORES)))
    out = np.stack([res.results[b]["out"] for b in range(NCORES)], axis=0)
    return out
